# version 33
# baseline (speedup 1.0000x reference)
import numpy as np
import ml_dtypes
from contextlib import ExitStack

import concourse.bass as bass
import concourse.mybir as mybir
from concourse.bass_utils import run_bass_kernel_spmd

F32 = mybir.dt.float32
BF16 = mybir.dt.bfloat16
I32 = mybir.dt.int32
AF = mybir.ActivationFunctionType
ALU = mybir.AluOpType
AX = mybir.AxisListType

B, L, D, NH, HD = 4, 4096, 1024, 16, 64
TQ = 2048
CH = 512
CW = 256
NJ = 4
DT = D // 128
HT = D // 128
NKT = L // 128
HID = 3072
ROPE_BASE = 10000.0
DEQ = 1.0 / 127.0

STARTS = {0: [0, 1536, 2048, 3584], 1: [512, 1024, 2560, 3072]}


def _host_tables(p):
    starts = STARTS[p]
    inv_ts = ROPE_BASE ** (-2.0 * np.arange(32, dtype=np.float64) / HD)

    bfd = ml_dtypes.bfloat16

    def tabs(pos):
        ang = np.outer(inv_ts, pos)
        return (np.tile(np.cos(ang), (4, 1)).astype(bfd),
                np.tile(np.sin(ang), (4, 1)).astype(bfd))

    cosk, sink = tabs(np.arange(L, dtype=np.float64))
    qpos = np.concatenate([np.arange(s, s + CH) for s in starts]).astype(np.float64)
    cosq, sinq = tabs(qpos)

    pm = np.zeros((128, 128), dtype=np.float32)
    for m in range(128):
        if (m % 64) < 32:
            pm[m + 32, m] = -1.0
        else:
            pm[m - 32, m] = 1.0
    idn = np.eye(128, dtype=np.float32)

    mk = np.zeros((128, NJ * 8 * CH), dtype=np.float32)
    for j in range(NJ):
        for t in range(8):
            kpos = 1024 * j + 128 * t + np.arange(128)
            qpos_j = starts[j] + np.arange(CH)
            mk[:, (8 * j + t) * CH:(8 * j + t + 1) * CH] = (
                kpos[:, None] <= qpos_j[None, :]).astype(np.float32)
    bf = ml_dtypes.bfloat16
    return (cosk, sink, cosq, sinq, pm.astype(bf), idn.astype(bf), mk.astype(bf))


def build_program(debug=()):
    nc = bass.Bass("TRN2", target_bir_lowering=False)
    dbg_names = [d[0] for d in debug]
    dd = {}

    def din(name, shape, dt):
        dd[name] = nc.dram_tensor(name, shape, dt, kind="ExternalInput").ap()

    din("xkv", [L, D], F32)
    din("xq", [TQ, D], F32)
    for w in ("wq", "wk", "wv", "wo"):
        din(w, [D, D], F32)
    for bnm in ("bq", "bk", "bv", "bo"):
        din(bnm, [1, D], F32)
    I8 = mybir.dt.int8
    din("wgT", [HID, D], I8)
    din("wuT", [HID, D], I8)
    din("wdT", [HID, D], I8)
    din("cosk", [128, L], BF16)
    din("sink", [128, L], BF16)
    din("cosq", [128, TQ], BF16)
    din("sinq", [128, TQ], BF16)
    din("pmat", [128, 128], BF16)
    din("idn", [128, 128], BF16)
    din("mask01", [128, NJ * 8 * CH], BF16)
    d_xnq = nc.dram_tensor("xnq_spill", [128, DT * TQ], BF16, kind="Internal").ap()
    d_out = nc.dram_tensor("out", [TQ, D], F32, kind="ExternalOutput").ap()
    d_dbg = {nm: nc.dram_tensor("dbg_" + nm, shape, dt, kind="ExternalOutput").ap()
             for nm, shape, dt in debug}

    C = {}
    LASTI = {}

    class NS:
        pass
    S = NS()

    def inc(instr, sem, k=1):
        C[sem.name] = C.get(sem.name, 0) + k
        instr.then_inc(sem, k)
        return C[sem.name]

    def cur(sem):
        return C.get(sem.name, 0)

    ENG2SEM = {}

    def T(instr):
        LASTI["pe"] = [instr, False]
        return instr

    def V(instr):
        LASTI["ve"] = [instr, False]
        return instr

    def Sc(instr):
        LASTI["se"] = [instr, False]
        return instr

    def Ti(instr, k=1):
        LASTI["pe"] = [instr, True]
        return inc(instr, S.pe, k)

    def Vi(instr, k=1):
        LASTI["ve"] = [instr, True]
        return inc(instr, S.ve, k)

    def Si(instr, k=1):
        LASTI["se"] = [instr, True]
        return inc(instr, S.se, k)

    with ExitStack() as top:
        E = top.enter_context
        for snm in ("dP", "dX0", "dX1", "dT", "dM0", "dM1", "dG", "dO",
                    "dW0", "dW1", "pe", "ve", "se"):
            setattr(S, snm, E(nc.semaphore(snm)))
        DX = [S.dX0, S.dX1]
        DM = [S.dM0, S.dM1]
        DW = [S.dW0, S.dW1]

        def VW():
            nc.vector.wait_ge(S.ve, cur(S.ve))

        def SW():
            nc.scalar.wait_ge(S.se, cur(S.se))

        def TW():
            nc.tensor.wait_ge(S.pe, cur(S.pe))

        def gdma(out_ap, in_ap):
            nc.gpsimd.wait_ge(S.dG, cur(S.dG))
            return inc(nc.gpsimd.dma_start(out_ap, in_ap), S.dG, 16)
        ENG2SEM.update({"pe": S.pe, "ve": S.ve, "se": S.se})

        def barrier():
            for key in ("pe", "ve", "se"):
                li = LASTI.get(key)
                if li is not None and not li[1]:
                    inc(li[0], ENG2SEM[key])
                    li[1] = True
            vals = [(ENG2SEM[k], cur(ENG2SEM[k])) for k in ("pe", "ve", "se")]
            for eng in (nc.tensor, nc.vector, nc.scalar, nc.sync, nc.gpsimd):
                for sem, v in vals:
                    if v:
                        eng.wait_ge(sem, v)

        sQC = E(nc.sbuf_tensor("sQC", [128, HT * TQ], BF16))
        sPmat = E(nc.sbuf_tensor("sPmat", [128, 128], BF16))
        sIdn = E(nc.sbuf_tensor("sIdn", [128, 128], BF16))
        sOnesR = E(nc.sbuf_tensor("sOnesR", [1, CH], BF16))
        sOnesC = E(nc.sbuf_tensor("sOnesC", [128, 1], BF16))
        sStat = E(nc.sbuf_tensor("sStat", [128, 32], F32))
        sBiasC = E(nc.sbuf_tensor("sBiasC", [128, 1], F32))
        sBq = E(nc.sbuf_tensor("sBq", [1, D], BF16))
        sBk = E(nc.sbuf_tensor("sBk", [1, D], BF16))
        sBv = E(nc.sbuf_tensor("sBv", [1, D], BF16))
        sBo = E(nc.sbuf_tensor("sBo", [1, D], BF16))

        inc(nc.sync.dma_start(sPmat[:, :], dd["pmat"]), S.dP, 16)
        inc(nc.sync.dma_start(sIdn[:, :], dd["idn"]), S.dP, 16)
        for t, bnm in ((sBq, "bq"), (sBk, "bk"), (sBv, "bv"), (sBo, "bo")):
            gdma(t[:, :], dd[bnm])
        persist_dP, persist_dG = cur(S.dP), cur(S.dG)
        nc.vector.wait_ge(S.dP, persist_dP)
        nc.tensor.wait_ge(S.dP, persist_dP)
        nc.scalar.wait_ge(S.dP, persist_dP)
        V(nc.vector.memset(sOnesR[:, :], 1.0))
        V(nc.vector.memset(sOnesC[:, :], 1.0))
        Vi(nc.vector.memset(sStat[:, :], 0.0))

        def ln_stage1(sx_blk, sxn_blk, col, dma_val):
            m0 = sStat[:, col + 0:col + 1]
            m2 = sStat[:, col + 2:col + 3]
            nc.scalar.wait_ge(dma_val[0], dma_val[1])
            SW()
            nc.scalar.wait_ge(S.ve, cur(S.ve))
            nc.scalar.wait_ge(S.pe, C.get("_sxn_pe", 0))
            sq = Si(nc.scalar.activation(sxn_blk, sx_blk, AF.Square, accum_out=m2))
            nc.vector.wait_ge(dma_val[0], dma_val[1])
            VW()
            C.setdefault("_wck", []).append(
                (S.ve, Vi(nc.vector.reduce_sum(m0, sx_blk, AX.X))))
            return sq

        def ln_stage2(col, sq):
            m0 = sStat[:, col + 0:col + 1]
            m1 = sStat[:, col + 1:col + 2]
            m2 = sStat[:, col + 2:col + 3]
            m4 = sStat[:, col + 4:col + 5]
            nc.vector.wait_ge(S.se, sq)
            VW()
            Vi(nc.vector.tensor_scalar_mul(m1, m0, 1.0 / D))
            Vi(nc.vector.tensor_scalar_mul(m2, m2, 1.0 / D))
            VW()
            Vi(nc.vector.tensor_tensor(m4, m1, m1, ALU.mult))
            VW()
            Vi(nc.vector.tensor_tensor(m2, m2, m4, ALU.subtract))
            VW()
            return Vi(nc.vector.tensor_scalar_add(m2, m2, 1e-5))

        def ln_stage3(col, v):
            m2 = sStat[:, col + 2:col + 3]
            m3 = sStat[:, col + 3:col + 4]
            nc.scalar.wait_ge(S.ve, v)
            SW()
            Sc2 = Si(nc.scalar.activation(m3, m2, AF.Ln))
            nc.scalar.wait_ge(S.se, Sc2)
            return Si(nc.scalar.activation(m3, m3, AF.Exp, scale=-0.5))

        def ln_stage4(sx_blk, sxn_blk, col, sc):
            m1 = sStat[:, col + 1:col + 2]
            m3 = sStat[:, col + 3:col + 4]
            nc.vector.wait_ge(S.se, sc)
            VW()
            return Vi(nc.vector.tensor_scalar(sxn_blk, sx_blk, m1, m3,
                                              ALU.subtract, ALU.mult))

        def ln_pair(items):
            st = [ln_stage1(*it) for it in items]
            st = [ln_stage2(it[2], q) for it, q in zip(items, st)]
            st = [ln_stage3(it[2], v) for it, v in zip(items, st)]
            out = [ln_stage4(it[0], it[1], it[2], sc) for it, sc in zip(items, st)]
            C.setdefault("_wck", []).extend((S.ve, v) for v in out)
            return out[-1]

        def ln_chunk(dsrc, c, sx, sxn, cw):
            PC = cw // 128
            tok0 = cw * c
            for s0 in range(0, PC, 2):
                items = []
                for s in (s0, s0 + 1):
                    slot = (c * PC + s) % 2
                    xb = sx[:, slot * D:(slot + 1) * D]
                    if c * PC + s >= 2:
                        nc.sync.wait_ge(S.ve, C.get("_lnv", 0))
                    dv = inc(nc.sync.dma_start(
                        xb, dsrc[tok0 + 128 * s: tok0 + 128 * (s + 1), :]), DX[slot], 16)
                    C.setdefault("_wck", []).append((DX[slot], dv))
                    items.append((xb, sxn[:, s * D:(s + 1) * D], 5 * slot, (DX[slot], dv)))
                C["_lnv"] = ln_pair(items)

        def tr_chunk(c, sxn, sxnT, pTr, cw, spill=None):
            PC = cw // 128
            wl = C.pop("_wck", [])
            if wl:
                nc.tensor.wait_ge(S.ve, C.get("_trev", 0))
                prev = None
                for sem, val in wl:
                    if (sem, val) == prev:
                        continue
                    prev = (sem, val)
                    nc.tensor.wait_ge(sem, val)
                    T(nc.tensor.transpose(pTr[1][0:1, 0:1], sIdn[0:1, 0:1],
                                          sIdn[0:1, 0:1]))
            nc.tensor.wait_ge(S.ve, C["_lnv"])
            if spill is not None and c >= 1:
                nc.vector.wait_ge(S.dO, C.get("_spv", 0))
            dpb = 8 // PC
            for i4 in range(DT // dpb):
                bank = pTr[i4 % 2]
                if c * (DT // dpb) + i4 >= 2:
                    nc.tensor.wait_ge(S.ve, C.get("_trev", 0))
                last = None
                for ii in range(dpb):
                    i = dpb * i4 + ii
                    for s in range(PC):
                        last = T(nc.tensor.transpose(
                            bank[:, cw * ii + 128 * s: cw * ii + 128 * (s + 1)],
                            sxn[:, s * D + 128 * i: s * D + 128 * (i + 1)],
                            sIdn[:, :]))
                pv = Ti(last)
                C["_sxn_pe"] = pv
                nc.vector.wait_ge(S.pe, pv)
                C["_trev"] = Vi(nc.vector.tensor_copy(
                    sxnT[:, cw * dpb * i4: cw * dpb * (i4 + 1)], bank[:, :]))
            nc.tensor.wait_ge(S.ve, C["_trev"])

        def build_proj_phase(dsrc, n_tok, w_sb, w_val, bias_row,
                             cos_dram, sin_dram, dest, dest_stride, bufs, cw, do_v=None,
                             spill=None):
            (sx, sxn, sxnT, sct, sst, stmp, srk1, srk2, pK, pSh, pTr) = bufs
            pend_v = [None]

            def emit_vev(kt, half, mmv):
                bank = (pK[2] if half == 0 else pSh[2])
                nc.vector.wait_ge(S.pe, mmv)
                vdst = sV[:, 1040 * kt + 520 * half: 1040 * kt + 520 * (half + 1)]
                C["_vev%d" % half] = Vi(nc.vector.tensor_copy(
                    vdst.rearrange("p (h e) -> p h e", e=65)[:, :, 0:64],
                    bank[:, 0:CH].rearrange("p (h e) -> p h e", e=64)))
                if half == 1:
                    vv = sV[:, 1040 * kt: 1040 * (kt + 1)]
                    C["_vev1"] = Vi(nc.vector.memset(
                        vv.rearrange("p (h e) -> p h e", e=65)[:, :, 64:65], 1.0))

            lnq = {}

            def ln_emit(c):
                ln_chunk(dsrc, c, sx, sxn, cw)
                lnq[c] = C["_lnv"]

            ln_emit(0)
            for c in range(n_tok // cw):
                tok0 = cw * c
                C["_lnv"] = lnq[c]
                tr_chunk(c, sxn, sxnT, pTr, cw, spill)
                if c + 1 < n_tok // cw:
                    ln_emit(c + 1)
                sl = c % 2
                sctv = sct[:, cw * sl: cw * (sl + 1)]
                sstv = sst[:, cw * sl: cw * (sl + 1)]
                if c >= 2:
                    nc.sync.wait_ge(S.ve, C.get("_rpslot%d" % sl, 0))
                ctv = inc(nc.sync.dma_start(sctv, cos_dram[:, tok0:tok0 + cw]), DM[sl], 16)
                stv = inc(nc.sync.dma_start(sstv, sin_dram[:, tok0:tok0 + cw]), DM[sl], 16)
                nc.tensor.wait_ge(S.dG, w_val)
                nc.vector.wait_ge(DM[sl], stv)

                def emit_rope(i, mmv):
                    par = i % 3
                    bank = pK[par]
                    shb = pSh[par]
                    stp = stmp[:, cw * par: cw * (par + 1)]
                    srA = srk1[:, 0:cw]
                    srC = srk2[:, 0:cw]
                    nc.scalar.wait_ge(S.pe, mmv)
                    nc.scalar.wait_ge(S.pe, C.get("_kshp%d" % par, 0))
                    nc.scalar.wait_ge(S.ve, C.get("_rpv%d" % par, 0))
                    C["_kevs%d" % par] = Si(nc.scalar.activation(stp, bank[:, 0:cw], AF.Copy))
                    nc.tensor.wait_ge(S.se, C["_kevs%d" % par])
                    nc.tensor.wait_ge(S.ve, C.get("_rpv%d" % par, 0))
                    if par == 2:
                        nc.tensor.wait_ge(S.ve, C.get("_vev1", 0))
                    C["_kshp%d" % par] = Ti(nc.tensor.matmul(shb[:, 0:cw], sPmat[:, :], stp,
                                                             start=True, stop=True))
                    nc.vector.wait_ge(S.pe, C["_kshp%d" % par])
                    VW()
                    Vi(nc.vector.tensor_tensor(srC, stp, sctv, ALU.mult))
                    Vi(nc.vector.tensor_tensor(srA, shb[:, 0:cw], sstv, ALU.mult))
                    VW()
                    C["_rpv%d" % par] = Vi(nc.vector.tensor_tensor(
                        dest[:, dest_stride * i + tok0: dest_stride * i + tok0 + cw],
                        srC, srA, ALU.add))
                    C["_rpv"] = C["_rpv%d" % par]

                pend = []
                if do_v is not None and pend_v[0] is not None:
                    emit_vev(*pend_v[0])
                    pend_v[0] = None
                for i in range(HT):
                    bank = pK[i % 3]
                    if c * HT + i >= 3:
                        nc.tensor.wait_ge(S.se, C.get("_kevs%d" % (i % 3), 0))
                    if do_v is not None and i % 3 == 2:
                        nc.tensor.wait_ge(S.ve, C.get("_vev0", 0))
                    for dt2 in range(DT):
                        T(nc.tensor.matmul(bank[:, 0:cw],
                                           w_sb[:, D * dt2 + 128 * i: D * dt2 + 128 * (i + 1)],
                                           sxnT[:, cw * dt2: cw * (dt2 + 1)],
                                           start=(dt2 == 0), stop=False))
                    pv = Ti(nc.tensor.matmul(bank[:, 0:cw], bias_row[:, 128 * i:128 * (i + 1)],
                                             sOnesR[:, 0:cw], start=False, stop=True))
                    if len(pend) >= 2:
                        emit_rope(*pend.pop(0))
                    pend.append((i, pv))
                if do_v is None:
                    while pend:
                        emit_rope(*pend.pop(0))
                    if spill is not None:
                        nc.sync.wait_ge(S.pe, cur(S.pe))
                        C["_spv"] = inc(nc.sync.dma_start(
                            spill[:, :].rearrange("p (t q) -> p t q", t=DT)
                                 [:, :, cw * c: cw * (c + 1)],
                            sxnT[:, :].rearrange("p (t q) -> p t q", t=DT)), S.dO, 16)
                else:
                    swv_sb, swv_val = do_v
                    nc.tensor.wait_ge(S.dG, swv_val)
                    PC = cw // 128
                    for s2 in range(PC):
                        kt = PC * c + s2
                        for half in range(2):
                            bank = (pK[2] if half == 0 else pSh[2])
                            if half == 0:
                                nc.tensor.wait_ge(S.se, C.get("_kevs2", 0))
                            else:
                                nc.tensor.wait_ge(S.ve, C.get("_rpv2", 0))
                            nc.tensor.wait_ge(S.ve, C.get("_vev%d" % half, 0))
                            for dt2 in range(DT):
                                T(nc.tensor.matmul(
                                    bank[:, 0:CH],
                                    sxnT[:, cw * dt2 + 128 * s2: cw * dt2 + 128 * (s2 + 1)],
                                    swv_sb[:, D * dt2 + CH * half: D * dt2 + CH * (half + 1)],
                                    start=(dt2 == 0), stop=False))
                            pv = Ti(nc.tensor.matmul(bank[:, 0:CH], sOnesR[:, 0:128],
                                                     sBv[:, CH * half: CH * (half + 1)],
                                                     start=False, stop=True))
                            if pend:
                                emit_rope(*pend.pop(0))
                            if pend_v[0] is not None:
                                emit_vev(*pend_v[0])
                            if s2 == PC - 1 and half == 1:
                                emit_vev(kt, half, pv)
                                pend_v[0] = None
                            else:
                                pend_v[0] = (kt, half, pv)
                C["_rpslot%d" % sl] = C.get("_rpv", 0)
            if pend_v[0] is not None:
                emit_vev(*pend_v[0])

        def stab_norms(dest, dest_stride, n_tok, statcol, sSq2, pNm):
            ncc = n_tok // CW
            sq_se = {}
            mm_pe = {}
            red_ve = {}

            def emit_sq(k):
                cc, i = divmod(k, HT)
                par = k % 3
                nc.scalar.wait_ge(S.pe, mm_pe.get(k - 3, 0))
                sq_se[k] = Si(nc.scalar.activation(
                    sSq2[:, CW * par: CW * (par + 1)],
                    dest[:, dest_stride * i + CW * cc: dest_stride * i + CW * (cc + 1)],
                    AF.Square))

            def emit_mm(k):
                cc, i = divmod(k, HT)
                par = k % 3
                nc.tensor.wait_ge(S.se, sq_se[k])
                if i == 0:
                    nc.tensor.wait_ge(S.ve, red_ve.get(cc - 1, 0))
                mm_pe[k] = Ti(nc.tensor.matmul(
                    pNm[:, 0:CW], sOnesC[:, :], sSq2[:, CW * par: CW * (par + 1)],
                    start=(i == 0), stop=(i == HT - 1)))
                if i == HT - 1:
                    nc.vector.wait_ge(S.pe, mm_pe[k])
                    VW()
                    Vi(nc.vector.reduce_max(sStat[0:1, 16:17], pNm[:, 0:CW], AX.X))
                    VW()
                    red_ve[cc] = Vi(nc.vector.tensor_tensor(
                        sStat[0:1, statcol:statcol + 1], sStat[0:1, 16:17],
                        sStat[0:1, statcol:statcol + 1], ALU.max))

            for k in range(ncc * HT):
                emit_sq(k)
                if k > 1:
                    emit_mm(k - 2)
            emit_mm(ncc * HT - 2)
            emit_mm(ncc * HT - 1)
            nc.tensor.wait_ge(S.ve, red_ve[ncc - 1])
            nc.scalar.wait_ge(S.ve, red_ve[ncc - 1])

        with ExitStack() as bd:
            Ebd = bd.enter_context
            sKT = Ebd(nc.sbuf_tensor("sKT", [128, HT * L], BF16))
            sV = Ebd(nc.sbuf_tensor("sV", [128, NKT * (NH * 65)], BF16))

            with ExitStack() as ph:
                ph.enter_context(nc.named_scope("phB1_KT"))
                Eb = ph.enter_context
                CWB = 512
                swk = Eb(nc.sbuf_tensor("swk", [128, DT * D], BF16))
                sx = Eb(nc.sbuf_tensor("sx", [128, 2 * D], F32))
                sct = Eb(nc.sbuf_tensor("sct", [128, 2 * CWB], BF16))
                sst = Eb(nc.sbuf_tensor("sst", [128, 2 * CWB], BF16))
                stmp = Eb(nc.sbuf_tensor("stmp", [128, 3 * CWB], BF16))
                srk1 = Eb(nc.sbuf_tensor("srk1", [128, CWB], BF16))
                srk2 = Eb(nc.sbuf_tensor("srk2", [128, CWB], BF16))
                pK = [Eb(nc.psum_tensor(f"pK{i}", [128, CH], F32)) for i in range(3)]
                pSh = [Eb(nc.psum_tensor(f"pSh{i}", [128, CH], F32)) for i in range(3)]
                pTr = [Eb(nc.psum_tensor(f"pTr{i}", [128, 1024], BF16)) for i in range(2)]
                pNm = pSh[0][0:1, :]
                swv = sQC[:, 0:DT * D]
                sxnT = sQC[:, DT * D: DT * D + DT * CWB]
                sxn = sQC[:, DT * D + DT * CWB: DT * D + DT * CWB + 4 * D]

                wk_v = gdma(
                    swk[:, :].rearrange("p (t h) -> p t h", t=DT),
                    dd["wk"].rearrange("(t p) h -> p t h", p=128))
                wv_v = gdma(
                    swv.rearrange("p (t h) -> p t h", t=DT),
                    dd["wv"].rearrange("(t p) h -> p t h", p=128))

                bufs = (sx, sxn, sxnT, sct, sst, stmp, srk1, srk2, pK, pSh, pTr)
                build_proj_phase(dd["xkv"], L, swk, wk_v, sBk,
                                 dd["cosk"], dd["sink"], sKT, L, bufs, CWB,
                                 do_v=(swv, wv_v))
                barrier()
                stab_norms(sKT, L, L, 17, stmp[:, 0:3 * CW], pNm)
                barrier()

            with ExitStack() as ph:
                ph.enter_context(nc.named_scope("phC_QT"))
                Ec = ph.enter_context
                swq = Ec(nc.sbuf_tensor("swq", [128, DT * D], BF16))
                sx = Ec(nc.sbuf_tensor("sxC", [128, 2 * D], F32))
                sxn = Ec(nc.sbuf_tensor("sxnC", [128, 2 * D], BF16))
                sxnT = Ec(nc.sbuf_tensor("sxnTC", [128, DT * CW], BF16))
                sct = Ec(nc.sbuf_tensor("sctC", [128, 2 * CW], BF16))
                sst = Ec(nc.sbuf_tensor("sstC", [128, 2 * CW], BF16))
                stmp = Ec(nc.sbuf_tensor("stmpC", [128, 3 * CW], BF16))
                srk1 = Ec(nc.sbuf_tensor("srk1C", [128, CW], BF16))
                srk2 = Ec(nc.sbuf_tensor("srk2C", [128, CW], BF16))
                pK = [Ec(nc.psum_tensor(f"pQ{i}", [128, CH], F32)) for i in range(3)]
                pSh = [Ec(nc.psum_tensor(f"pShQ{i}", [128, CH], F32)) for i in range(3)]
                pTr = [Ec(nc.psum_tensor(f"pTrQ{i}", [128, 1024], BF16)) for i in range(2)]
                pNm = pSh[0][0:1, :]
                pBC = pSh[1]

                wq_v = gdma(
                    swq[:, :].rearrange("p (t h) -> p t h", t=DT),
                    dd["wq"].rearrange("(t p) h -> p t h", p=128))

                bufs = (sx, sxn, sxnT, sct, sst, stmp, srk1, srk2, pK, pSh, pTr)
                build_proj_phase(dd["xq"], TQ, swq, wq_v, sBq,
                                 dd["cosq"], dd["sinq"], sQC, TQ, bufs, CW,
                                 spill=d_xnq)
                barrier()
                stab_norms(sQC, TQ, TQ, 18, stmp[:, 0:3 * CW], pNm)
                VW()
                vv = Vi(nc.vector.tensor_tensor(sStat[0:1, 19:20], sStat[0:1, 17:18],
                                                sStat[0:1, 18:19], ALU.mult))
                nc.scalar.wait_ge(S.ve, vv)
                SW()
                sv = Si(nc.scalar.sqrt(sStat[0:1, 20:21], sStat[0:1, 19:20]))
                nc.vector.wait_ge(S.se, sv)
                VW()
                Vi(nc.vector.tensor_scalar_mul(sStat[0:1, 21:22], sStat[0:1, 20:21], -0.125))
                VW()
                vv = Vi(nc.vector.tensor_copy(stmp[0:1, 0:1], sStat[0:1, 21:22]))
                nc.tensor.wait_ge(S.ve, vv)
                pv = Ti(nc.tensor.matmul(pBC[:, 0:1], sOnesR[:, 0:128], stmp[0:1, 0:1],
                                         start=True, stop=True))
                nc.vector.wait_ge(S.pe, pv)
                Vi(nc.vector.tensor_copy(sBiasC[:, :], pBC[:, 0:1]))
                barrier()

            for nm, buf in (("KT", sKT), ("V", sV), ("QT", sQC), ("biasC", sBiasC)):
                if nm in dbg_names:
                    inc(nc.sync.dma_start(d_dbg[nm], buf[:, :]), S.dO, 16)
            if dbg_names:
                for eng in (nc.sync, nc.tensor, nc.vector, nc.scalar):
                    eng.wait_ge(S.dO, cur(S.dO))

            with ExitStack() as ph:
                ph.enter_context(nc.named_scope("phD_attn"))
                Ed = ph.enter_context
                sMk = [Ed(nc.sbuf_tensor(f"sMk{i}", [128, 8 * CH], BF16)) for i in range(2)]
                sPTB = Ed(nc.sbuf_tensor("sPTB", [128, 6 * 1024], BF16))
                sPT = [sPTB[:, 1024 * i:1024 * (i + 1)] for i in range(6)]
                sSums = Ed(nc.sbuf_tensor("sSums", [1, 1024], F32))
                sR64 = Ed(nc.sbuf_tensor("sR64", [64, 1024], F32))
                pSB = Ed(nc.psum_tensor("pSB", [128, 3 * 1024], F32))
                pS = [pSB[:, 1024 * i:1024 * (i + 1)] for i in range(3)]
                pCA = Ed(nc.psum_tensor("pCA", [65, CH], F32))
                pCB = Ed(nc.psum_tensor("pCB", [65, CH], F32))
                sOnesCF = Ed(nc.sbuf_tensor("sOnesCF", [1, 64], F32))
                for eng in (nc.sync, nc.vector, nc.scalar):
                    eng.wait_ge(S.dO, cur(S.dO))
                Vi(nc.vector.memset(sOnesCF[:, :], 1.0))

                mk_done = {}
                mk_done[0] = inc(nc.sync.dma_start(sMk[0][:, :], dd["mask01"][:, 0:8 * CH]),
                                 DM[0], 16)
                gkt = -1
                CL = 4
                NPT = 6
                exp_se = {}
                mask_ve = {}
                ctx_pe = {}
                exp_meta = {}
                sc_pe = {}
                exp_pend = [None]
                tail_ve = [0]
                pend_ctx = []
                pend_tail = [None]

                def emit_ctx(p):
                    kt, g, pb, hA, hB, KTl = p
                    if mask_ve[g]:
                        nc.tensor.wait_ge(S.ve, mask_ve[g])
                    else:
                        nc.tensor.wait_ge(S.se, exp_se[g])
                    if kt == 0 and tail_ve[0]:
                        nc.tensor.wait_ge(S.ve, tail_ve[0])
                    T(nc.tensor.matmul(
                        pCA[:, :],
                        sV[:, 1040 * kt + 65 * hA: 1040 * kt + 65 * hA + 65],
                        pb[:, 0:CH], start=(kt == 0), stop=(kt == KTl - 1)))
                    cv = Ti(nc.tensor.matmul(
                        pCB[:, :],
                        sV[:, 1040 * kt + 65 * hB: 1040 * kt + 65 * hB + 65],
                        pb[:, CH:1024], start=(kt == 0), stop=(kt == KTl - 1)))
                    ctx_pe[g] = cv
                    return cv

                def tail_early(qc, cv, j):
                    nc.vector.wait_ge(S.pe, cv)
                    VW()
                    Vi(nc.vector.tensor_copy(sSums[:, 0:CH], pCA[64:65, :]))
                    Vi(nc.vector.tensor_copy(sSums[:, CH:1024], pCB[64:65, :]))
                    Vi(nc.vector.tensor_copy(sR64[:, 0:CH], pCA[0:64, :]))
                    tail_ve[0] = Vi(nc.vector.tensor_copy(sR64[:, CH:1024],
                                                          pCB[0:64, :]))
                    VW()
                    return Vi(nc.vector.reciprocal(sSums[:, :], sSums[:, :]))

                def tail_late(qc, sv, j, pBc):
                    nc.tensor.wait_ge(S.ve, sv)
                    T(nc.tensor.matmul(pBc[:, 0:CH], sOnesCF[:, :], sSums[:, 0:CH],
                                       start=True, stop=True))
                    pv = Ti(nc.tensor.matmul(pBc[:, CH:1024], sOnesCF[:, :],
                                             sSums[:, CH:1024], start=True, stop=True))
                    nc.vector.wait_ge(S.pe, pv)
                    VW()
                    Vi(nc.vector.tensor_tensor(sQC[0:64, qc: qc + CH],
                                               sR64[:, 0:CH], pBc[:, 0:CH], ALU.mult))
                    vv = Vi(nc.vector.tensor_tensor(sQC[64:128, qc: qc + CH],
                                                    sR64[:, CH:1024],
                                                    pBc[:, CH:1024], ALU.mult))
                    return vv

                for j in range(NJ):
                    KTl = 8 * (j + 1)
                    if j + 1 < NJ:
                        if j >= 1:
                            nc.sync.wait_ge(S.ve, C.get("_mklast%d" % (j - 1), 0))
                        mk_done[j + 1] = inc(nc.sync.dma_start(
                            sMk[(j + 1) % 2][:, :],
                            dd["mask01"][:, (j + 1) * 8 * CH:(j + 2) * 8 * CH]), DM[(j + 1) % 2], 16)
                    nc.vector.wait_ge(DM[j % 2], mk_done[j])
                    nc.scalar.wait_ge(DM[j % 2], mk_done[j])
                    mkb = sMk[j % 2]
                    for P in range(NH // 2):
                        hA, hB = 2 * P, 2 * P + 1
                        qc = TQ * P + CH * j
                        for kt in range(KTl):
                            gkt += 1
                            sb = pS[gkt % 3]
                            pb = sPT[gkt % NPT]
                            if gkt >= 3:
                                nc.tensor.wait_ge(S.se, exp_se[gkt - 3])
                            bg = C.get("_bcg")
                            if bg is not None and gkt == bg[0]:
                                nc.tensor.wait_ge(S.ve, bg[1])
                                C["_bcg"] = None
                            T(nc.tensor.matmul(
                                sb[:, 0:CH],
                                sKT[0:64, L * P + 128 * kt: L * P + 128 * (kt + 1)],
                                sQC[0:64, qc: qc + CH],
                                start=True, stop=True, tile_position=(0, 0)))
                            pv = Ti(nc.tensor.matmul(
                                sb[:, CH:1024],
                                sKT[64:128, L * P + 128 * kt: L * P + 128 * (kt + 1)],
                                sQC[64:128, qc: qc + CH],
                                start=True, stop=True, tile_position=(64, 0)))
                            def post_exp(g):
                                kt_g, j_g = exp_meta[g]
                                if kt_g >= 8 * (j_g + 1) - 8:
                                    wt = kt_g - 8 * j_g
                                    nc.vector.wait_ge(S.se, exp_se[g])
                                    pbg = sPT[g % NPT]
                                    mslg = sMk[j_g % 2][:, wt * CH:(wt + 1) * CH]
                                    V(nc.vector.tensor_tensor(pbg[:, 0:CH], pbg[:, 0:CH],
                                                              mslg, ALU.mult))
                                    mask_ve[g] = Vi(nc.vector.tensor_tensor(
                                        pbg[:, CH:1024], pbg[:, CH:1024], mslg, ALU.mult))
                                else:
                                    mask_ve[g] = 0

                            def emit_exp(g0, g1):
                                nc.scalar.wait_ge(S.pe, sc_pe[g1])
                                for g in (g0, g1):
                                    if g >= NPT:
                                        nc.scalar.wait_ge(S.pe, ctx_pe.get(g - NPT, 0))
                                        if mask_ve.get(g - NPT, 0):
                                            nc.scalar.wait_ge(S.ve, mask_ve[g - NPT])
                                n = g1 - g0 + 1
                                ev = Si(nc.scalar.activation(
                                    sPTB[:, 1024 * (g0 % NPT): 1024 * (g0 % NPT) + 1024 * n],
                                    pSB[:, 1024 * (g0 % 3): 1024 * (g0 % 3) + 1024 * n],
                                    AF.Exp, bias=sBiasC[:, 0:1], scale=0.125))
                                for g in range(g0, g1 + 1):
                                    exp_se[g] = ev
                                    post_exp(g)

                            exp_meta[gkt] = (kt, j)
                            sc_pe[gkt] = pv
                            if False:
                                exp_pend[0] = gkt
                            elif exp_pend[0] is not None:
                                g0 = exp_pend[0]
                                exp_pend[0] = None
                                emit_exp(g0, gkt)
                            else:
                                emit_exp(gkt, gkt)
                            if kt == 7 and pend_tail[0] is not None:
                                _, qc0, sv0, j0 = pend_tail[0]
                                nc.tensor.wait_ge(S.se, exp_se[gkt - 1])
                                tgt = (gkt + 2) % 3
                                bv = tail_late(qc0, sv0, j0, pS[tgt][0:64, :])
                                C["_bcg"] = (gkt + 2, bv)
                                pend_tail[0] = None
                            if len(pend_ctx) > CL:
                                emit_ctx(pend_ctx.pop(0))
                            pend_ctx.append((kt, gkt, pb, hA, hB, KTl))
                        while pend_ctx:
                            cv = emit_ctx(pend_ctx.pop(0))
                        pend_tail[0] = ("late", qc, tail_early(qc, cv, j), j)
                    C["_mklast%d" % j] = mask_ve[gkt]
                _, qc0, sv0, j0 = pend_tail[0]
                tail_late(qc0, sv0, j0, pS[0][0:64, :])
                pend_tail[0] = None
                barrier()

        if "ctxT" in dbg_names:
            inc(nc.sync.dma_start(d_dbg["ctxT"], sQC[:, :]), S.dO, 16)
            for eng in (nc.sync, nc.tensor, nc.vector, nc.scalar):
                eng.wait_ge(S.dO, cur(S.dO))

        with ExitStack() as ph:
            ph.enter_context(nc.named_scope("phE_ffn"))
            Ee = ph.enter_context
            sOut = Ee(nc.sbuf_tensor("sOut", [128, 16 * D], F32))
            sxnTq = Ee(nc.sbuf_tensor("sxnTq", [128, DT * TQ], BF16))
            swo = Ee(nc.sbuf_tensor("swo", [128, DT * D], BF16))
            wo_v = gdma(
                swo[:, :].rearrange("p (t h) -> p t h", t=DT),
                dd["wo"].rearrange("(t p) h -> p t h", p=128))
            with ExitStack() as ph1:
                E1 = ph1.enter_context
                sAct = E1(nc.sbuf_tensor("sAct", [128, 6 * TQ], BF16))
                swgT = E1(nc.sbuf_tensor("swgT", [128, 2 * D], BF16))
                swuT = E1(nc.sbuf_tensor("swuT", [128, 2 * D], BF16))
                swdT = E1(nc.sbuf_tensor("swdT", [128, 6 * D], BF16))
                ssg = E1(nc.sbuf_tensor("ssg", [128, 2 * CH], BF16))
                pG = [E1(nc.psum_tensor(f"pG{i}", [128, CH], F32)) for i in range(3)]
                pU = [E1(nc.psum_tensor(f"pU{i}", [128, CH], F32)) for i in range(3)]
                pD = [E1(nc.psum_tensor(f"pD{i}", [128, CH], F32)) for i in range(2)]

                xq_v = inc(nc.sync.dma_start(sxnTq[:, :], d_xnq), S.dT, 16)
                nc.tensor.wait_ge(S.dT, xq_v)
                sd_v = inc(nc.sync.dma_start(
                    sOut[:, :].rearrange("p (t d) -> p t d", t=16),
                    dd["xq"].rearrange("(t p) d -> p t d", p=128)), DX[0], 16)
                C["_seed_dma"] = sd_v
                nc.vector.wait_ge(DX[0], sd_v)
                barrier()

                gu_pe = {0: 0, 1: 0}
                dn_grp_pe = 0
                act_se = {}
                act_ve = {}
                pend_act = None
                pend_dn = None

                def emit_act(h6, cq, gcq, pv, grp):
                    slot = gcq % 2
                    nc.scalar.wait_ge(S.pe, pv)
                    nc.scalar.wait_ge(S.ve, act_ve.get(("s", slot), 0))
                    sv = Si(nc.scalar.activation(ssg[:, CH * slot: CH * (slot + 1)],
                                                 pG[gcq % 3][:, :], AF.Silu, scale=DEQ))
                    act_se[gcq % 3] = sv
                    nc.vector.wait_ge(S.se, sv)
                    if grp > 0:
                        nc.vector.wait_ge(S.pe, dn_grp_pe)
                    vv = Vi(nc.vector.tensor_tensor(
                        sAct[:, TQ * h6 + CH * cq: TQ * h6 + CH * (cq + 1)],
                        ssg[:, CH * slot: CH * (slot + 1)], pU[gcq % 3][:, :], ALU.mult))
                    act_ve[gcq % 3] = vv
                    act_ve[("s", slot)] = vv
                    C["_actv"] = vv

                def emit_dn(tt, dh, pv):
                    nc.vector.wait_ge(S.pe, pv)
                    VW()
                    C["_dnv"] = Vi(nc.vector.scalar_tensor_tensor(
                        sOut[:, D * tt + CH * dh: D * tt + CH * (dh + 1)],
                        pD[(2 * tt + dh) % 2][:, :], DEQ * DEQ,
                        sOut[:, D * tt + CH * dh: D * tt + CH * (dh + 1)],
                        ALU.mult, ALU.add))

                pre_uv = {}
                for grp in range(4):
                    for h6 in range(6):
                        ht = 6 * grp + h6
                        par = ht % 2
                        if h6 == 0 and grp > 0:
                            uv = pre_uv.pop(grp)
                            nc.gpsimd.wait_ge(S.pe, dn_grp_pe)
                        else:
                            if ht >= 2:
                                nc.gpsimd.wait_ge(S.pe, gu_pe[par])
                            gv = inc(nc.gpsimd.dma_start(
                                swgT[:, D * par: D * (par + 1)],
                                dd["wgT"][128 * ht:128 * (ht + 1), :]), DW[par], 16)
                            uv = inc(nc.gpsimd.dma_start(
                                swuT[:, D * par: D * (par + 1)],
                                dd["wuT"][128 * ht:128 * (ht + 1), :]), DW[par], 16)
                        dv = inc(nc.gpsimd.dma_start(
                            swdT[:, D * h6: D * (h6 + 1)],
                            dd["wdT"][128 * ht:128 * (ht + 1), :]), DW[par], 16)
                        if h6 == 0 and grp > 0:
                            nc.tensor.wait_ge(S.dP, uv)
                        else:
                            nc.tensor.wait_ge(DW[par], dv)
                        for cq in range(TQ // CH):
                            gcq = ht * 4 + cq
                            slot = gcq % 3
                            nc.tensor.wait_ge(S.se, act_se.get(slot, 0))
                            nc.tensor.wait_ge(S.ve, act_ve.get(slot, 0))
                            for dt2 in range(DT):
                                T(nc.tensor.matmul(
                                    pG[slot][:, :],
                                    swgT[:, D * par + 128 * dt2: D * par + 128 * (dt2 + 1)],
                                    sxnTq[:, TQ * dt2 + CH * cq: TQ * dt2 + CH * (cq + 1)],
                                    start=(dt2 == 0), stop=(dt2 == DT - 1)))
                            last = None
                            for dt2 in range(DT):
                                last = T(nc.tensor.matmul(
                                    pU[slot][:, :],
                                    swuT[:, D * par + 128 * dt2: D * par + 128 * (dt2 + 1)],
                                    sxnTq[:, TQ * dt2 + CH * cq: TQ * dt2 + CH * (cq + 1)],
                                    start=(dt2 == 0), stop=(dt2 == DT - 1)))
                            pv = Ti(last)
                            gu_pe[par] = pv
                            if pend_act is not None:
                                emit_act(*pend_act)
                            pend_act = (h6, cq, gcq, pv, grp)
                    emit_act(*pend_act)
                    pend_act = None
                    if grp + 1 < 4:
                        ht2 = 6 * (grp + 1)
                        nc.gpsimd.wait_ge(S.pe, gu_pe[0])
                        inc(nc.gpsimd.dma_start(
                            swgT[:, 0:D],
                            dd["wgT"][128 * ht2:128 * (ht2 + 1), :]), S.dP, 16)
                        pre_uv[grp + 1] = inc(nc.gpsimd.dma_start(
                            swuT[:, 0:D],
                            dd["wuT"][128 * ht2:128 * (ht2 + 1), :]), S.dP, 16)
                    nc.tensor.wait_ge(S.ve, C["_actv"])
                    for tt in range(TQ // 128):
                        for dh in range(2):
                            idx = 2 * tt + dh
                            bank = pD[idx % 2]
                            if grp * 32 + idx >= 2:
                                nc.tensor.wait_ge(S.ve, C.get("_dnv", 0))
                            last = None
                            for h6 in range(6):
                                last = T(nc.tensor.matmul(
                                    bank[:, :],
                                    sAct[:, TQ * h6 + 128 * tt: TQ * h6 + 128 * (tt + 1)],
                                    swdT[:, D * h6 + CH * dh: D * h6 + CH * (dh + 1)],
                                    start=(h6 == 0), stop=(h6 == 5)))
                            pv = Ti(last)
                            dn_grp_pe = pv
                            if pend_dn is not None:
                                emit_dn(*pend_dn)
                            pend_dn = (tt, dh, pv)
                    emit_dn(*pend_dn)
                    pend_dn = None
                barrier()

            if "ffnout" in dbg_names:
                inc(nc.sync.dma_start(d_dbg["ffnout"], sOut[:, :]), S.dO, 16)
                for eng in (nc.sync, nc.tensor, nc.vector, nc.scalar):
                    eng.wait_ge(S.dO, cur(S.dO))

            with ExitStack() as phf:
                phf.enter_context(nc.named_scope("phF_outproj"))
                Ef = phf.enter_context
                pO = [Ef(nc.psum_tensor(f"pO{i}", [128, CH], F32)) for i in range(4)]
                nc.tensor.wait_ge(S.dG, wo_v)
                for tt in range(TQ // 128):
                    for dh in range(2):
                        bidx = (2 * tt + dh) % 4
                        bank = pO[bidx]
                        if 2 * tt + dh >= 4:
                            nc.tensor.wait_ge(S.ve, C.get("_opv%d" % bidx, 0))
                        for i in range(HT):
                            T(nc.tensor.matmul(
                                bank[:, :],
                                sQC[:, TQ * i + 128 * tt: TQ * i + 128 * (tt + 1)],
                                swo[:, D * i + CH * dh: D * i + CH * (dh + 1)],
                                start=(i == 0), stop=False))
                        pv = Ti(nc.tensor.matmul(bank[:, :], sOnesR[:, 0:128],
                                                 sBo[:, CH * dh: CH * (dh + 1)],
                                                 start=False, stop=True))
                        nc.vector.wait_ge(S.pe, pv)
                        VW()
                        C["_opv"] = C["_opv%d" % bidx] = Vi(nc.vector.tensor_tensor(
                            sOut[:, D * tt + CH * dh: D * tt + CH * (dh + 1)],
                            sOut[:, D * tt + CH * dh: D * tt + CH * (dh + 1)],
                            bank[:, :], ALU.add))
                        if dh == 1:
                            nc.sync.wait_ge(S.ve, C["_opv"])
                            inc(nc.sync.dma_start(d_out[128 * tt:128 * (tt + 1), :],
                                                  sOut[:, D * tt: D * (tt + 1)]), S.dO, 16)
                nc.sync.wait_ge(S.dO, cur(S.dO))
                nc.tensor.wait_ge(S.dO, cur(S.dO))

    return nc


def kernel(L0_attn_x, L0_attn_wq_w, L0_attn_wq_b, L0_attn_wk_w, L0_attn_wk_b,
           L0_attn_wv_w, L0_attn_wv_b, L0_attn_wo_w, L0_attn_wo_b,
           L0_ffn_w_gate, L0_ffn_w_up, L0_ffn_w_down, _debug=(), _cores=None):
    x = np.ascontiguousarray(np.asarray(L0_attn_x, dtype=np.float32))
    f32c = lambda a: np.ascontiguousarray(np.asarray(a, dtype=np.float32))
    i8c = lambda a: np.ascontiguousarray(np.asarray(a).astype(np.int8))
    wq, wk, wv, wo = map(f32c, (L0_attn_wq_w, L0_attn_wk_w, L0_attn_wv_w, L0_attn_wo_w))
    bq, bk, bv, bo = (f32c(b).reshape(1, D) for b in
                      (L0_attn_wq_b, L0_attn_wk_b, L0_attn_wv_b, L0_attn_wo_b))
    def tile_gu(w):
        a = i8c(np.asarray(w).T).reshape(DT, 128, HID // 128, 128)
        return np.ascontiguousarray(a.transpose(2, 1, 0, 3).reshape(HID, D))
    wgT = tile_gu(L0_ffn_w_gate)
    wuT = tile_gu(L0_ffn_w_up)
    wdT = i8c(np.asarray(L0_ffn_w_down).T)

    tables = {p: _host_tables(p) for p in (0, 1)}
    nc = build_program(debug=_debug)
    in_maps = []
    cores = _cores if _cores is not None else list(range(8))
    for core in cores:
        b, p = core // 2, core % 2
        cosk, sink, cosq, sinq, pm, idn, mk = tables[p]
        starts = STARTS[p]
        xq = np.ascontiguousarray(np.concatenate([x[b, s:s + CH] for s in starts], 0))
        in_maps.append({
            "xkv": np.ascontiguousarray(x[b]), "xq": xq,
            "wq": wq, "wk": wk, "wv": wv, "wo": wo,
            "bq": bq, "bk": bk, "bv": bv, "bo": bo,
            "wgT": wgT, "wuT": wuT, "wdT": wdT,
            "cosk": cosk, "sink": sink, "cosq": cosq, "sinq": sinq,
            "pmat": pm, "idn": idn, "mask01": mk,
        })
    res = run_bass_kernel_spmd(nc, in_maps, core_ids=cores)
    if _debug:
        return res
    out = np.empty((B, L, D), dtype=np.float32)
    for ci, core in enumerate(cores):
        b, p = core // 2, core % 2
        o = res.results[ci]["out"]
        for j, s in enumerate(STARTS[p]):
            out[b, s:s + CH] = o[CH * j: CH * (j + 1)]
    return out


# version 34 (final)
# speedup vs baseline: 1.0047x; 1.0047x over previous
import numpy as np
import ml_dtypes
from contextlib import ExitStack

import concourse.bass as bass
import concourse.mybir as mybir
from concourse.bass_utils import run_bass_kernel_spmd

F32 = mybir.dt.float32
BF16 = mybir.dt.bfloat16
I32 = mybir.dt.int32
AF = mybir.ActivationFunctionType
ALU = mybir.AluOpType
AX = mybir.AxisListType

B, L, D, NH, HD = 4, 4096, 1024, 16, 64
TQ = 2048
CH = 512
CW = 256
NJ = 4
DT = D // 128
HT = D // 128
NKT = L // 128
HID = 3072
ROPE_BASE = 10000.0
DEQ = 1.0 / 127.0

STARTS = {0: [0, 1536, 2048, 3584], 1: [512, 1024, 2560, 3072]}


def _host_tables(p):
    starts = STARTS[p]
    inv_ts = ROPE_BASE ** (-2.0 * np.arange(32, dtype=np.float64) / HD)

    bfd = ml_dtypes.bfloat16

    def tabs(pos):
        ang = np.outer(inv_ts, pos)
        return (np.tile(np.cos(ang), (4, 1)).astype(bfd),
                np.tile(np.sin(ang), (4, 1)).astype(bfd))

    cosk, sink = tabs(np.arange(L, dtype=np.float64))
    qpos = np.concatenate([np.arange(s, s + CH) for s in starts]).astype(np.float64)
    cosq, sinq = tabs(qpos)

    pm = np.zeros((128, 128), dtype=np.float32)
    for m in range(128):
        if (m % 64) < 32:
            pm[m + 32, m] = -1.0
        else:
            pm[m - 32, m] = 1.0
    idn = np.eye(128, dtype=np.float32)

    mk = np.zeros((128, NJ * 8 * CH), dtype=np.float32)
    for j in range(NJ):
        for t in range(8):
            kpos = 1024 * j + 128 * t + np.arange(128)
            qpos_j = starts[j] + np.arange(CH)
            mk[:, (8 * j + t) * CH:(8 * j + t + 1) * CH] = (
                kpos[:, None] <= qpos_j[None, :]).astype(np.float32)
    bf = ml_dtypes.bfloat16
    return (cosk, sink, cosq, sinq, pm.astype(bf), idn.astype(bf), mk.astype(bf))


def build_program(debug=()):
    nc = bass.Bass("TRN2", target_bir_lowering=False)
    dbg_names = [d[0] for d in debug]
    dd = {}

    def din(name, shape, dt):
        dd[name] = nc.dram_tensor(name, shape, dt, kind="ExternalInput").ap()

    din("xkv", [L, D], F32)
    din("xq", [TQ, D], F32)
    for w in ("wq", "wk", "wv", "wo"):
        din(w, [D, D], F32)
    for bnm in ("bq", "bk", "bv", "bo"):
        din(bnm, [1, D], F32)
    I8 = mybir.dt.int8
    din("wgT", [HID, D], I8)
    din("wuT", [HID, D], I8)
    din("wdT", [HID, D], I8)
    din("cosk", [128, L], BF16)
    din("sink", [128, L], BF16)
    din("cosq", [128, TQ], BF16)
    din("sinq", [128, TQ], BF16)
    din("pmat", [128, 128], BF16)
    din("idn", [128, 128], BF16)
    din("mask01", [128, NJ * 8 * CH], BF16)
    d_xnq = nc.dram_tensor("xnq_spill", [128, DT * TQ], BF16, kind="Internal").ap()
    d_out = nc.dram_tensor("out", [TQ, D], F32, kind="ExternalOutput").ap()
    d_dbg = {nm: nc.dram_tensor("dbg_" + nm, shape, dt, kind="ExternalOutput").ap()
             for nm, shape, dt in debug}

    C = {}
    LASTI = {}

    class NS:
        pass
    S = NS()

    def inc(instr, sem, k=1):
        C[sem.name] = C.get(sem.name, 0) + k
        instr.then_inc(sem, k)
        return C[sem.name]

    def cur(sem):
        return C.get(sem.name, 0)

    ENG2SEM = {}

    def T(instr):
        LASTI["pe"] = [instr, False]
        return instr

    def V(instr):
        LASTI["ve"] = [instr, False]
        return instr

    def Sc(instr):
        LASTI["se"] = [instr, False]
        return instr

    def Ti(instr, k=1):
        LASTI["pe"] = [instr, True]
        return inc(instr, S.pe, k)

    def Vi(instr, k=1):
        LASTI["ve"] = [instr, True]
        return inc(instr, S.ve, k)

    def Si(instr, k=1):
        LASTI["se"] = [instr, True]
        return inc(instr, S.se, k)

    with ExitStack() as top:
        E = top.enter_context
        for snm in ("dP", "dX0", "dX1", "dT", "dM0", "dM1", "dG", "dO",
                    "dW0", "dW1", "dF", "pe", "ve", "se"):
            setattr(S, snm, E(nc.semaphore(snm)))
        DX = [S.dX0, S.dX1]
        DM = [S.dM0, S.dM1]
        DW = [S.dW0, S.dW1]

        def VW():
            nc.vector.wait_ge(S.ve, cur(S.ve))

        def SW():
            nc.scalar.wait_ge(S.se, cur(S.se))

        def TW():
            nc.tensor.wait_ge(S.pe, cur(S.pe))

        def gdma(out_ap, in_ap):
            nc.gpsimd.wait_ge(S.dG, cur(S.dG))
            return inc(nc.gpsimd.dma_start(out_ap, in_ap), S.dG, 16)
        ENG2SEM.update({"pe": S.pe, "ve": S.ve, "se": S.se})

        def barrier():
            for key in ("pe", "ve", "se"):
                li = LASTI.get(key)
                if li is not None and not li[1]:
                    inc(li[0], ENG2SEM[key])
                    li[1] = True
            vals = [(ENG2SEM[k], cur(ENG2SEM[k])) for k in ("pe", "ve", "se")]
            for eng in (nc.tensor, nc.vector, nc.scalar, nc.sync, nc.gpsimd):
                for sem, v in vals:
                    if v:
                        eng.wait_ge(sem, v)

        sQC = E(nc.sbuf_tensor("sQC", [128, HT * TQ], BF16))
        sPmat = E(nc.sbuf_tensor("sPmat", [128, 128], BF16))
        sIdn = E(nc.sbuf_tensor("sIdn", [128, 128], BF16))
        sOnesR = E(nc.sbuf_tensor("sOnesR", [1, CH], BF16))
        sOnesC = E(nc.sbuf_tensor("sOnesC", [128, 1], BF16))
        sStat = E(nc.sbuf_tensor("sStat", [128, 32], F32))
        sBiasC = E(nc.sbuf_tensor("sBiasC", [128, 1], F32))
        sBq = E(nc.sbuf_tensor("sBq", [1, D], BF16))
        sBk = E(nc.sbuf_tensor("sBk", [1, D], BF16))
        sBv = E(nc.sbuf_tensor("sBv", [1, D], BF16))
        sBo = E(nc.sbuf_tensor("sBo", [1, D], BF16))

        inc(nc.sync.dma_start(sPmat[:, :], dd["pmat"]), S.dP, 16)
        inc(nc.sync.dma_start(sIdn[:, :], dd["idn"]), S.dP, 16)
        for t, bnm in ((sBq, "bq"), (sBk, "bk"), (sBv, "bv"), (sBo, "bo")):
            gdma(t[:, :], dd[bnm])
        persist_dP, persist_dG = cur(S.dP), cur(S.dG)
        nc.vector.wait_ge(S.dP, persist_dP)
        nc.tensor.wait_ge(S.dP, persist_dP)
        nc.scalar.wait_ge(S.dP, persist_dP)
        V(nc.vector.memset(sOnesR[:, :], 1.0))
        V(nc.vector.memset(sOnesC[:, :], 1.0))
        Vi(nc.vector.memset(sStat[:, :], 0.0))

        def ln_stage1(sx_blk, sxn_blk, col, dma_val):
            m0 = sStat[:, col + 0:col + 1]
            m2 = sStat[:, col + 2:col + 3]
            nc.scalar.wait_ge(dma_val[0], dma_val[1])
            SW()
            nc.scalar.wait_ge(S.ve, cur(S.ve))
            nc.scalar.wait_ge(S.pe, C.get("_sxn_pe", 0))
            sq = Si(nc.scalar.activation(sxn_blk, sx_blk, AF.Square, accum_out=m2))
            nc.vector.wait_ge(dma_val[0], dma_val[1])
            VW()
            C.setdefault("_wck", []).append(
                (S.ve, Vi(nc.vector.reduce_sum(m0, sx_blk, AX.X))))
            return sq

        def ln_stage2(col, sq):
            m0 = sStat[:, col + 0:col + 1]
            m1 = sStat[:, col + 1:col + 2]
            m2 = sStat[:, col + 2:col + 3]
            m4 = sStat[:, col + 4:col + 5]
            nc.vector.wait_ge(S.se, sq)
            VW()
            Vi(nc.vector.tensor_scalar_mul(m1, m0, 1.0 / D))
            Vi(nc.vector.tensor_scalar_mul(m2, m2, 1.0 / D))
            VW()
            Vi(nc.vector.tensor_tensor(m4, m1, m1, ALU.mult))
            VW()
            Vi(nc.vector.tensor_tensor(m2, m2, m4, ALU.subtract))
            VW()
            return Vi(nc.vector.tensor_scalar_add(m2, m2, 1e-5))

        def ln_stage3(col, v):
            m2 = sStat[:, col + 2:col + 3]
            m3 = sStat[:, col + 3:col + 4]
            nc.scalar.wait_ge(S.ve, v)
            SW()
            Sc2 = Si(nc.scalar.activation(m3, m2, AF.Ln))
            nc.scalar.wait_ge(S.se, Sc2)
            return Si(nc.scalar.activation(m3, m3, AF.Exp, scale=-0.5))

        def ln_stage4(sx_blk, sxn_blk, col, sc):
            m1 = sStat[:, col + 1:col + 2]
            m3 = sStat[:, col + 3:col + 4]
            nc.vector.wait_ge(S.se, sc)
            VW()
            return Vi(nc.vector.tensor_scalar(sxn_blk, sx_blk, m1, m3,
                                              ALU.subtract, ALU.mult))

        def ln_pair(items):
            st = [ln_stage1(*it) for it in items]
            st = [ln_stage2(it[2], q) for it, q in zip(items, st)]
            st = [ln_stage3(it[2], v) for it, v in zip(items, st)]
            out = [ln_stage4(it[0], it[1], it[2], sc) for it, sc in zip(items, st)]
            C.setdefault("_wck", []).extend((S.ve, v) for v in out)
            return out[-1]

        def ln_chunk(dsrc, c, sx, sxn, cw):
            PC = cw // 128
            tok0 = cw * c
            for s0 in range(0, PC, 2):
                items = []
                for s in (s0, s0 + 1):
                    slot = (c * PC + s) % 2
                    xb = sx[:, slot * D:(slot + 1) * D]
                    if c * PC + s >= 2:
                        nc.sync.wait_ge(S.ve, C.get("_lnv", 0))
                    dv = inc(nc.sync.dma_start(
                        xb, dsrc[tok0 + 128 * s: tok0 + 128 * (s + 1), :]), DX[slot], 16)
                    C.setdefault("_wck", []).append((DX[slot], dv))
                    items.append((xb, sxn[:, s * D:(s + 1) * D], 5 * slot, (DX[slot], dv)))
                C["_lnv"] = ln_pair(items)

        def tr_chunk(c, sxn, sxnT, pTr, cw, spill=None):
            PC = cw // 128
            wl = C.pop("_wck", [])
            if wl:
                nc.tensor.wait_ge(S.ve, C.get("_trev", 0))
                prev = None
                for sem, val in wl:
                    if (sem, val) == prev:
                        continue
                    prev = (sem, val)
                    nc.tensor.wait_ge(sem, val)
                    T(nc.tensor.transpose(pTr[1][0:1, 0:1], sIdn[0:1, 0:1],
                                          sIdn[0:1, 0:1]))
            nc.tensor.wait_ge(S.ve, C["_lnv"])
            if spill is not None and c >= 1:
                nc.vector.wait_ge(S.dO, C.get("_spv", 0))
            dpb = 8 // PC
            for i4 in range(DT // dpb):
                bank = pTr[i4 % 2]
                if c * (DT // dpb) + i4 >= 2:
                    nc.tensor.wait_ge(S.ve, C.get("_trev", 0))
                last = None
                for ii in range(dpb):
                    i = dpb * i4 + ii
                    for s in range(PC):
                        last = T(nc.tensor.transpose(
                            bank[:, cw * ii + 128 * s: cw * ii + 128 * (s + 1)],
                            sxn[:, s * D + 128 * i: s * D + 128 * (i + 1)],
                            sIdn[:, :]))
                pv = Ti(last)
                C["_sxn_pe"] = pv
                nc.vector.wait_ge(S.pe, pv)
                C["_trev"] = Vi(nc.vector.tensor_copy(
                    sxnT[:, cw * dpb * i4: cw * dpb * (i4 + 1)], bank[:, :]))
            nc.tensor.wait_ge(S.ve, C["_trev"])

        def build_proj_phase(dsrc, n_tok, w_sb, w_val, bias_row,
                             cos_dram, sin_dram, dest, dest_stride, bufs, cw, do_v=None,
                             spill=None):
            (sx, sxn, sxnT, sct, sst, stmp, srk1, srk2, pK, pSh, pTr) = bufs
            pend_v = [None]

            def emit_vev(kt, half, mmv):
                bank = (pK[2] if half == 0 else pSh[2])
                nc.vector.wait_ge(S.pe, mmv)
                vdst = sV[:, 1040 * kt + 520 * half: 1040 * kt + 520 * (half + 1)]
                C["_vev%d" % half] = Vi(nc.vector.tensor_copy(
                    vdst.rearrange("p (h e) -> p h e", e=65)[:, :, 0:64],
                    bank[:, 0:CH].rearrange("p (h e) -> p h e", e=64)))
                if half == 1:
                    vv = sV[:, 1040 * kt: 1040 * (kt + 1)]
                    C["_vev1"] = Vi(nc.vector.memset(
                        vv.rearrange("p (h e) -> p h e", e=65)[:, :, 64:65], 1.0))

            lnq = {}

            def ln_emit(c):
                ln_chunk(dsrc, c, sx, sxn, cw)
                lnq[c] = C["_lnv"]

            ln_emit(0)
            for c in range(n_tok // cw):
                tok0 = cw * c
                C["_lnv"] = lnq[c]
                tr_chunk(c, sxn, sxnT, pTr, cw, spill)
                if c + 1 < n_tok // cw:
                    ln_emit(c + 1)
                sl = c % 2
                sctv = sct[:, cw * sl: cw * (sl + 1)]
                sstv = sst[:, cw * sl: cw * (sl + 1)]
                if c >= 2:
                    nc.sync.wait_ge(S.ve, C.get("_rpslot%d" % sl, 0))
                ctv = inc(nc.sync.dma_start(sctv, cos_dram[:, tok0:tok0 + cw]), DM[sl], 16)
                stv = inc(nc.sync.dma_start(sstv, sin_dram[:, tok0:tok0 + cw]), DM[sl], 16)
                nc.tensor.wait_ge(S.dG, w_val)
                nc.vector.wait_ge(DM[sl], stv)

                def emit_rope(i, mmv):
                    par = i % 3
                    bank = pK[par]
                    shb = pSh[par]
                    stp = stmp[:, cw * par: cw * (par + 1)]
                    srA = srk1[:, 0:cw]
                    srC = srk2[:, 0:cw]
                    nc.scalar.wait_ge(S.pe, mmv)
                    nc.scalar.wait_ge(S.pe, C.get("_kshp%d" % par, 0))
                    nc.scalar.wait_ge(S.ve, C.get("_rpv%d" % par, 0))
                    C["_kevs%d" % par] = Si(nc.scalar.activation(stp, bank[:, 0:cw], AF.Copy))
                    nc.tensor.wait_ge(S.se, C["_kevs%d" % par])
                    nc.tensor.wait_ge(S.ve, C.get("_rpv%d" % par, 0))
                    if par == 2:
                        nc.tensor.wait_ge(S.ve, C.get("_vev1", 0))
                    C["_kshp%d" % par] = Ti(nc.tensor.matmul(shb[:, 0:cw], sPmat[:, :], stp,
                                                             start=True, stop=True))
                    nc.vector.wait_ge(S.pe, C["_kshp%d" % par])
                    VW()
                    Vi(nc.vector.tensor_tensor(srC, stp, sctv, ALU.mult))
                    Vi(nc.vector.tensor_tensor(srA, shb[:, 0:cw], sstv, ALU.mult))
                    VW()
                    C["_rpv%d" % par] = Vi(nc.vector.tensor_tensor(
                        dest[:, dest_stride * i + tok0: dest_stride * i + tok0 + cw],
                        srC, srA, ALU.add))
                    C["_rpv"] = C["_rpv%d" % par]

                pend = []
                if do_v is not None and pend_v[0] is not None:
                    emit_vev(*pend_v[0])
                    pend_v[0] = None
                for i in range(HT):
                    bank = pK[i % 3]
                    if c * HT + i >= 3:
                        nc.tensor.wait_ge(S.se, C.get("_kevs%d" % (i % 3), 0))
                    if do_v is not None and i % 3 == 2:
                        nc.tensor.wait_ge(S.ve, C.get("_vev0", 0))
                    for dt2 in range(DT):
                        T(nc.tensor.matmul(bank[:, 0:cw],
                                           w_sb[:, D * dt2 + 128 * i: D * dt2 + 128 * (i + 1)],
                                           sxnT[:, cw * dt2: cw * (dt2 + 1)],
                                           start=(dt2 == 0), stop=False))
                    pv = Ti(nc.tensor.matmul(bank[:, 0:cw], bias_row[:, 128 * i:128 * (i + 1)],
                                             sOnesR[:, 0:cw], start=False, stop=True))
                    if len(pend) >= 2:
                        emit_rope(*pend.pop(0))
                    pend.append((i, pv))
                if do_v is None:
                    while pend:
                        emit_rope(*pend.pop(0))
                    if spill is not None:
                        nc.sync.wait_ge(S.pe, cur(S.pe))
                        C["_spv"] = inc(nc.sync.dma_start(
                            spill[:, :].rearrange("p (t q) -> p t q", t=DT)
                                 [:, :, cw * c: cw * (c + 1)],
                            sxnT[:, :].rearrange("p (t q) -> p t q", t=DT)), S.dO, 16)
                else:
                    swv_sb, swv_val = do_v
                    nc.tensor.wait_ge(S.dG, swv_val)
                    PC = cw // 128
                    for s2 in range(PC):
                        kt = PC * c + s2
                        for half in range(2):
                            bank = (pK[2] if half == 0 else pSh[2])
                            if half == 0:
                                nc.tensor.wait_ge(S.se, C.get("_kevs2", 0))
                            else:
                                nc.tensor.wait_ge(S.ve, C.get("_rpv2", 0))
                            nc.tensor.wait_ge(S.ve, C.get("_vev%d" % half, 0))
                            for dt2 in range(DT):
                                T(nc.tensor.matmul(
                                    bank[:, 0:CH],
                                    sxnT[:, cw * dt2 + 128 * s2: cw * dt2 + 128 * (s2 + 1)],
                                    swv_sb[:, D * dt2 + CH * half: D * dt2 + CH * (half + 1)],
                                    start=(dt2 == 0), stop=False))
                            pv = Ti(nc.tensor.matmul(bank[:, 0:CH], sOnesR[:, 0:128],
                                                     sBv[:, CH * half: CH * (half + 1)],
                                                     start=False, stop=True))
                            if pend:
                                emit_rope(*pend.pop(0))
                            if pend_v[0] is not None:
                                emit_vev(*pend_v[0])
                            if s2 == PC - 1 and half == 1:
                                emit_vev(kt, half, pv)
                                pend_v[0] = None
                            else:
                                pend_v[0] = (kt, half, pv)
                C["_rpslot%d" % sl] = C.get("_rpv", 0)
            if pend_v[0] is not None:
                emit_vev(*pend_v[0])

        def stab_norms(dest, dest_stride, n_tok, statcol, sSq2, pNm):
            ncc = n_tok // CW
            sq_se = {}
            mm_pe = {}
            red_ve = {}

            def emit_sq(k):
                cc, i = divmod(k, HT)
                par = k % 3
                nc.scalar.wait_ge(S.pe, mm_pe.get(k - 3, 0))
                sq_se[k] = Si(nc.scalar.activation(
                    sSq2[:, CW * par: CW * (par + 1)],
                    dest[:, dest_stride * i + CW * cc: dest_stride * i + CW * (cc + 1)],
                    AF.Square))

            def emit_mm(k):
                cc, i = divmod(k, HT)
                par = k % 3
                nc.tensor.wait_ge(S.se, sq_se[k])
                if i == 0:
                    nc.tensor.wait_ge(S.ve, red_ve.get(cc - 1, 0))
                mm_pe[k] = Ti(nc.tensor.matmul(
                    pNm[:, 0:CW], sOnesC[:, :], sSq2[:, CW * par: CW * (par + 1)],
                    start=(i == 0), stop=(i == HT - 1)))
                if i == HT - 1:
                    nc.vector.wait_ge(S.pe, mm_pe[k])
                    VW()
                    Vi(nc.vector.reduce_max(sStat[0:1, 16:17], pNm[:, 0:CW], AX.X))
                    VW()
                    red_ve[cc] = Vi(nc.vector.tensor_tensor(
                        sStat[0:1, statcol:statcol + 1], sStat[0:1, 16:17],
                        sStat[0:1, statcol:statcol + 1], ALU.max))

            for k in range(ncc * HT):
                emit_sq(k)
                if k > 1:
                    emit_mm(k - 2)
            emit_mm(ncc * HT - 2)
            emit_mm(ncc * HT - 1)
            nc.tensor.wait_ge(S.ve, red_ve[ncc - 1])
            nc.scalar.wait_ge(S.ve, red_ve[ncc - 1])

        with ExitStack() as bd:
            Ebd = bd.enter_context
            sKT = Ebd(nc.sbuf_tensor("sKT", [128, HT * L], BF16))
            sV = Ebd(nc.sbuf_tensor("sV", [128, NKT * (NH * 65)], BF16))

            with ExitStack() as ph:
                ph.enter_context(nc.named_scope("phB1_KT"))
                Eb = ph.enter_context
                CWB = 512
                swk = Eb(nc.sbuf_tensor("swk", [128, DT * D], BF16))
                sx = Eb(nc.sbuf_tensor("sx", [128, 2 * D], F32))
                sct = Eb(nc.sbuf_tensor("sct", [128, 2 * CWB], BF16))
                sst = Eb(nc.sbuf_tensor("sst", [128, 2 * CWB], BF16))
                stmp = Eb(nc.sbuf_tensor("stmp", [128, 3 * CWB], BF16))
                srk1 = Eb(nc.sbuf_tensor("srk1", [128, CWB], BF16))
                srk2 = Eb(nc.sbuf_tensor("srk2", [128, CWB], BF16))
                pK = [Eb(nc.psum_tensor(f"pK{i}", [128, CH], F32)) for i in range(3)]
                pSh = [Eb(nc.psum_tensor(f"pSh{i}", [128, CH], F32)) for i in range(3)]
                pTr = [Eb(nc.psum_tensor(f"pTr{i}", [128, 1024], BF16)) for i in range(2)]
                pNm = pSh[0][0:1, :]
                swv = sQC[:, 0:DT * D]
                sxnT = sQC[:, DT * D: DT * D + DT * CWB]
                sxn = sQC[:, DT * D + DT * CWB: DT * D + DT * CWB + 4 * D]

                wk_v = gdma(
                    swk[:, :].rearrange("p (t h) -> p t h", t=DT),
                    dd["wk"].rearrange("(t p) h -> p t h", p=128))
                wv_v = gdma(
                    swv.rearrange("p (t h) -> p t h", t=DT),
                    dd["wv"].rearrange("(t p) h -> p t h", p=128))

                bufs = (sx, sxn, sxnT, sct, sst, stmp, srk1, srk2, pK, pSh, pTr)
                build_proj_phase(dd["xkv"], L, swk, wk_v, sBk,
                                 dd["cosk"], dd["sink"], sKT, L, bufs, CWB,
                                 do_v=(swv, wv_v))
                barrier()
                stab_norms(sKT, L, L, 17, stmp[:, 0:3 * CW], pNm)
                barrier()

            with ExitStack() as ph:
                ph.enter_context(nc.named_scope("phC_QT"))
                Ec = ph.enter_context
                swq = Ec(nc.sbuf_tensor("swq", [128, DT * D], BF16))
                sx = Ec(nc.sbuf_tensor("sxC", [128, 2 * D], F32))
                sxn = Ec(nc.sbuf_tensor("sxnC", [128, 2 * D], BF16))
                sxnT = Ec(nc.sbuf_tensor("sxnTC", [128, DT * CW], BF16))
                sct = Ec(nc.sbuf_tensor("sctC", [128, 2 * CW], BF16))
                sst = Ec(nc.sbuf_tensor("sstC", [128, 2 * CW], BF16))
                stmp = Ec(nc.sbuf_tensor("stmpC", [128, 3 * CW], BF16))
                srk1 = Ec(nc.sbuf_tensor("srk1C", [128, CW], BF16))
                srk2 = Ec(nc.sbuf_tensor("srk2C", [128, CW], BF16))
                pK = [Ec(nc.psum_tensor(f"pQ{i}", [128, CH], F32)) for i in range(3)]
                pSh = [Ec(nc.psum_tensor(f"pShQ{i}", [128, CH], F32)) for i in range(3)]
                pTr = [Ec(nc.psum_tensor(f"pTrQ{i}", [128, 1024], BF16)) for i in range(2)]
                pNm = pSh[0][0:1, :]
                pBC = pSh[1]

                wq_v = gdma(
                    swq[:, :].rearrange("p (t h) -> p t h", t=DT),
                    dd["wq"].rearrange("(t p) h -> p t h", p=128))

                bufs = (sx, sxn, sxnT, sct, sst, stmp, srk1, srk2, pK, pSh, pTr)
                build_proj_phase(dd["xq"], TQ, swq, wq_v, sBq,
                                 dd["cosq"], dd["sinq"], sQC, TQ, bufs, CW,
                                 spill=d_xnq)
                barrier()
                stab_norms(sQC, TQ, TQ, 18, stmp[:, 0:3 * CW], pNm)
                VW()
                vv = Vi(nc.vector.tensor_tensor(sStat[0:1, 19:20], sStat[0:1, 17:18],
                                                sStat[0:1, 18:19], ALU.mult))
                nc.scalar.wait_ge(S.ve, vv)
                SW()
                sv = Si(nc.scalar.sqrt(sStat[0:1, 20:21], sStat[0:1, 19:20]))
                nc.vector.wait_ge(S.se, sv)
                VW()
                Vi(nc.vector.tensor_scalar_mul(sStat[0:1, 21:22], sStat[0:1, 20:21], -0.125))
                VW()
                vv = Vi(nc.vector.tensor_copy(stmp[0:1, 0:1], sStat[0:1, 21:22]))
                nc.tensor.wait_ge(S.ve, vv)
                pv = Ti(nc.tensor.matmul(pBC[:, 0:1], sOnesR[:, 0:128], stmp[0:1, 0:1],
                                         start=True, stop=True))
                nc.vector.wait_ge(S.pe, pv)
                Vi(nc.vector.tensor_copy(sBiasC[:, :], pBC[:, 0:1]))
                barrier()

            for nm, buf in (("KT", sKT), ("V", sV), ("QT", sQC), ("biasC", sBiasC)):
                if nm in dbg_names:
                    inc(nc.sync.dma_start(d_dbg[nm], buf[:, :]), S.dO, 16)
            if dbg_names:
                for eng in (nc.sync, nc.tensor, nc.vector, nc.scalar):
                    eng.wait_ge(S.dO, cur(S.dO))

            with ExitStack() as ph:
                ph.enter_context(nc.named_scope("phD_attn"))
                Ed = ph.enter_context
                sMk = [Ed(nc.sbuf_tensor(f"sMk{i}", [128, 8 * CH], BF16)) for i in range(2)]
                sPTB = Ed(nc.sbuf_tensor("sPTB", [128, 6 * 1024], BF16))
                sPT = [sPTB[:, 1024 * i:1024 * (i + 1)] for i in range(6)]
                sSums = Ed(nc.sbuf_tensor("sSums", [1, 1024], F32))
                sR64 = Ed(nc.sbuf_tensor("sR64", [64, 1024], F32))
                pSB = Ed(nc.psum_tensor("pSB", [128, 3 * 1024], F32))
                pS = [pSB[:, 1024 * i:1024 * (i + 1)] for i in range(3)]
                pCA = Ed(nc.psum_tensor("pCA", [65, CH], F32))
                pCB = Ed(nc.psum_tensor("pCB", [65, CH], F32))
                sOnesCF = Ed(nc.sbuf_tensor("sOnesCF", [1, 64], F32))
                for eng in (nc.sync, nc.vector, nc.scalar):
                    eng.wait_ge(S.dO, cur(S.dO))
                Vi(nc.vector.memset(sOnesCF[:, :], 1.0))

                mk_done = {}
                mk_done[0] = inc(nc.sync.dma_start(sMk[0][:, :], dd["mask01"][:, 0:8 * CH]),
                                 DM[0], 16)
                gkt = -1
                CL = 4
                NPT = 6
                exp_se = {}
                mask_ve = {}
                ctx_pe = {}
                exp_meta = {}
                sc_pe = {}
                exp_pend = [None]
                tail_ve = [0]
                pend_ctx = []
                pend_tail = [None]

                def emit_ctx(p):
                    kt, g, pb, hA, hB, KTl = p
                    if mask_ve[g]:
                        nc.tensor.wait_ge(S.ve, mask_ve[g])
                    else:
                        nc.tensor.wait_ge(S.se, exp_se[g])
                    if kt == 0 and tail_ve[0]:
                        nc.tensor.wait_ge(S.ve, tail_ve[0])
                    T(nc.tensor.matmul(
                        pCA[:, :],
                        sV[:, 1040 * kt + 65 * hA: 1040 * kt + 65 * hA + 65],
                        pb[:, 0:CH], start=(kt == 0), stop=(kt == KTl - 1)))
                    cv = Ti(nc.tensor.matmul(
                        pCB[:, :],
                        sV[:, 1040 * kt + 65 * hB: 1040 * kt + 65 * hB + 65],
                        pb[:, CH:1024], start=(kt == 0), stop=(kt == KTl - 1)))
                    ctx_pe[g] = cv
                    return cv

                def tail_early(qc, cv, j):
                    nc.vector.wait_ge(S.pe, cv)
                    VW()
                    Vi(nc.vector.tensor_copy(sSums[:, 0:CH], pCA[64:65, :]))
                    Vi(nc.vector.tensor_copy(sSums[:, CH:1024], pCB[64:65, :]))
                    Vi(nc.vector.tensor_copy(sR64[:, 0:CH], pCA[0:64, :]))
                    tail_ve[0] = Vi(nc.vector.tensor_copy(sR64[:, CH:1024],
                                                          pCB[0:64, :]))
                    VW()
                    return Vi(nc.vector.reciprocal(sSums[:, :], sSums[:, :]))

                def tail_late(qc, sv, j, pBc):
                    nc.tensor.wait_ge(S.ve, sv)
                    T(nc.tensor.matmul(pBc[:, 0:CH], sOnesCF[:, :], sSums[:, 0:CH],
                                       start=True, stop=True))
                    pv = Ti(nc.tensor.matmul(pBc[:, CH:1024], sOnesCF[:, :],
                                             sSums[:, CH:1024], start=True, stop=True))
                    nc.vector.wait_ge(S.pe, pv)
                    VW()
                    Vi(nc.vector.tensor_tensor(sQC[0:64, qc: qc + CH],
                                               sR64[:, 0:CH], pBc[:, 0:CH], ALU.mult))
                    vv = Vi(nc.vector.tensor_tensor(sQC[64:128, qc: qc + CH],
                                                    sR64[:, CH:1024],
                                                    pBc[:, CH:1024], ALU.mult))
                    return vv

                for j in range(NJ):
                    KTl = 8 * (j + 1)
                    if j + 1 < NJ:
                        if j >= 1:
                            nc.sync.wait_ge(S.ve, C.get("_mklast%d" % (j - 1), 0))
                        mk_done[j + 1] = inc(nc.sync.dma_start(
                            sMk[(j + 1) % 2][:, :],
                            dd["mask01"][:, (j + 1) * 8 * CH:(j + 2) * 8 * CH]), DM[(j + 1) % 2], 16)
                    nc.vector.wait_ge(DM[j % 2], mk_done[j])
                    nc.scalar.wait_ge(DM[j % 2], mk_done[j])
                    mkb = sMk[j % 2]
                    for P in range(NH // 2):
                        hA, hB = 2 * P, 2 * P + 1
                        qc = TQ * P + CH * j
                        for kt in range(KTl):
                            gkt += 1
                            sb = pS[gkt % 3]
                            pb = sPT[gkt % NPT]
                            if gkt >= 3:
                                nc.tensor.wait_ge(S.se, exp_se[gkt - 3])
                            bg = C.get("_bcg")
                            if bg is not None and gkt == bg[0]:
                                nc.tensor.wait_ge(S.ve, bg[1])
                                C["_bcg"] = None
                            T(nc.tensor.matmul(
                                sb[:, 0:CH],
                                sKT[0:64, L * P + 128 * kt: L * P + 128 * (kt + 1)],
                                sQC[0:64, qc: qc + CH],
                                start=True, stop=True, tile_position=(0, 0)))
                            pv = Ti(nc.tensor.matmul(
                                sb[:, CH:1024],
                                sKT[64:128, L * P + 128 * kt: L * P + 128 * (kt + 1)],
                                sQC[64:128, qc: qc + CH],
                                start=True, stop=True, tile_position=(64, 0)))
                            def post_exp(g):
                                kt_g, j_g = exp_meta[g]
                                if kt_g >= 8 * (j_g + 1) - 8:
                                    wt = kt_g - 8 * j_g
                                    nc.vector.wait_ge(S.se, exp_se[g])
                                    pbg = sPT[g % NPT]
                                    mslg = sMk[j_g % 2][:, wt * CH:(wt + 1) * CH]
                                    V(nc.vector.tensor_tensor(pbg[:, 0:CH], pbg[:, 0:CH],
                                                              mslg, ALU.mult))
                                    mask_ve[g] = Vi(nc.vector.tensor_tensor(
                                        pbg[:, CH:1024], pbg[:, CH:1024], mslg, ALU.mult))
                                else:
                                    mask_ve[g] = 0

                            def emit_exp(g0, g1):
                                nc.scalar.wait_ge(S.pe, sc_pe[g1])
                                for g in (g0, g1):
                                    if g >= NPT:
                                        nc.scalar.wait_ge(S.pe, ctx_pe.get(g - NPT, 0))
                                        if mask_ve.get(g - NPT, 0):
                                            nc.scalar.wait_ge(S.ve, mask_ve[g - NPT])
                                n = g1 - g0 + 1
                                ev = Si(nc.scalar.activation(
                                    sPTB[:, 1024 * (g0 % NPT): 1024 * (g0 % NPT) + 1024 * n],
                                    pSB[:, 1024 * (g0 % 3): 1024 * (g0 % 3) + 1024 * n],
                                    AF.Exp, bias=sBiasC[:, 0:1], scale=0.125))
                                for g in range(g0, g1 + 1):
                                    exp_se[g] = ev
                                    post_exp(g)

                            exp_meta[gkt] = (kt, j)
                            sc_pe[gkt] = pv
                            if False:
                                exp_pend[0] = gkt
                            elif exp_pend[0] is not None:
                                g0 = exp_pend[0]
                                exp_pend[0] = None
                                emit_exp(g0, gkt)
                            else:
                                emit_exp(gkt, gkt)
                            if kt == 7 and pend_tail[0] is not None:
                                _, qc0, sv0, j0 = pend_tail[0]
                                nc.tensor.wait_ge(S.se, exp_se[gkt - 1])
                                tgt = (gkt + 2) % 3
                                bv = tail_late(qc0, sv0, j0, pS[tgt][0:64, :])
                                C["_bcg"] = (gkt + 2, bv)
                                pend_tail[0] = None
                            if len(pend_ctx) > CL:
                                emit_ctx(pend_ctx.pop(0))
                            pend_ctx.append((kt, gkt, pb, hA, hB, KTl))
                        while pend_ctx:
                            cv = emit_ctx(pend_ctx.pop(0))
                        pend_tail[0] = ("late", qc, tail_early(qc, cv, j), j)
                    C["_mklast%d" % j] = mask_ve[gkt]
                _, qc0, sv0, j0 = pend_tail[0]
                tail_late(qc0, sv0, j0, pS[0][0:64, :])
                pend_tail[0] = None
                barrier()

        if "ctxT" in dbg_names:
            inc(nc.sync.dma_start(d_dbg["ctxT"], sQC[:, :]), S.dO, 16)
            for eng in (nc.sync, nc.tensor, nc.vector, nc.scalar):
                eng.wait_ge(S.dO, cur(S.dO))

        with ExitStack() as ph:
            ph.enter_context(nc.named_scope("phE_ffn"))
            Ee = ph.enter_context
            sOut = Ee(nc.sbuf_tensor("sOut", [128, 16 * D], F32))
            sxnTq = Ee(nc.sbuf_tensor("sxnTq", [128, DT * TQ], BF16))
            swo = Ee(nc.sbuf_tensor("swo", [128, DT * D], BF16))
            wo_v = gdma(
                swo[:, :].rearrange("p (t h) -> p t h", t=DT),
                dd["wo"].rearrange("(t p) h -> p t h", p=128))
            with ExitStack() as ph1:
                E1 = ph1.enter_context
                sAct = E1(nc.sbuf_tensor("sAct", [128, 6 * TQ], BF16))
                swgT = E1(nc.sbuf_tensor("swgT", [128, 2 * D], BF16))
                swuT = E1(nc.sbuf_tensor("swuT", [128, 2 * D], BF16))
                swdT = E1(nc.sbuf_tensor("swdT", [128, 6 * D], BF16))
                ssg = E1(nc.sbuf_tensor("ssg", [128, 2 * CH], BF16))
                pG = [E1(nc.psum_tensor(f"pG{i}", [128, CH], F32)) for i in range(3)]
                pU = [E1(nc.psum_tensor(f"pU{i}", [128, CH], F32)) for i in range(3)]
                pD = [E1(nc.psum_tensor(f"pD{i}", [128, CH], F32)) for i in range(2)]

                xq_v = inc(nc.sync.dma_start(sxnTq[:, :], d_xnq), S.dT, 16)
                nc.tensor.wait_ge(S.dT, xq_v)
                sd_v = inc(nc.sync.dma_start(
                    sOut[:, :].rearrange("p (t d) -> p t d", t=16),
                    dd["xq"].rearrange("(t p) d -> p t d", p=128)), DX[0], 16)
                C["_seed_dma"] = sd_v
                nc.vector.wait_ge(DX[0], sd_v)
                barrier()

                gu_pe = {0: 0, 1: 0}
                dn_grp_pe = 0
                act_se = {}
                act_ve = {}
                pend_act = None
                pend_dn = None

                def emit_act(h6, cq, gcq, pv, grp):
                    slot = gcq % 2
                    nc.scalar.wait_ge(S.pe, pv)
                    nc.scalar.wait_ge(S.ve, act_ve.get(("s", slot), 0))
                    sv = Si(nc.scalar.activation(ssg[:, CH * slot: CH * (slot + 1)],
                                                 pG[gcq % 3][:, :], AF.Silu, scale=DEQ))
                    act_se[gcq % 3] = sv
                    nc.vector.wait_ge(S.se, sv)
                    if grp > 0:
                        nc.vector.wait_ge(S.pe, dn_grp_pe)
                    vv = Vi(nc.vector.tensor_tensor(
                        sAct[:, TQ * h6 + CH * cq: TQ * h6 + CH * (cq + 1)],
                        ssg[:, CH * slot: CH * (slot + 1)], pU[gcq % 3][:, :], ALU.mult))
                    act_ve[gcq % 3] = vv
                    act_ve[("s", slot)] = vv
                    C["_actv"] = vv

                def emit_dn(tt, dh, pv):
                    nc.vector.wait_ge(S.pe, pv)
                    VW()
                    C["_dnv"] = Vi(nc.vector.scalar_tensor_tensor(
                        sOut[:, D * tt + CH * dh: D * tt + CH * (dh + 1)],
                        pD[(2 * tt + dh) % 2][:, :], DEQ * DEQ,
                        sOut[:, D * tt + CH * dh: D * tt + CH * (dh + 1)],
                        ALU.mult, ALU.add))

                pre_uv = {}
                for grp in range(4):
                    for h6 in range(6):
                        ht = 6 * grp + h6
                        par = ht % 2
                        if h6 == 0 and grp > 0:
                            uv = pre_uv.pop(grp)
                            nc.gpsimd.wait_ge(S.pe, dn_grp_pe)
                        else:
                            if ht >= 2:
                                nc.gpsimd.wait_ge(S.pe, gu_pe[par])
                            gv = inc(nc.gpsimd.dma_start(
                                swgT[:, D * par: D * (par + 1)],
                                dd["wgT"][128 * ht:128 * (ht + 1), :]), DW[par], 16)
                            uv = inc(nc.gpsimd.dma_start(
                                swuT[:, D * par: D * (par + 1)],
                                dd["wuT"][128 * ht:128 * (ht + 1), :]), DW[par], 16)
                        dv = inc(nc.gpsimd.dma_start(
                            swdT[:, D * h6: D * (h6 + 1)],
                            dd["wdT"][128 * ht:128 * (ht + 1), :]), DW[par], 16)
                        if h6 == 0 and grp > 0:
                            nc.tensor.wait_ge(S.dF, uv)
                        else:
                            nc.tensor.wait_ge(DW[par], dv)
                        for cq in range(TQ // CH):
                            gcq = ht * 4 + cq
                            slot = gcq % 3
                            nc.tensor.wait_ge(S.se, act_se.get(slot, 0))
                            nc.tensor.wait_ge(S.ve, act_ve.get(slot, 0))
                            for dt2 in range(DT):
                                T(nc.tensor.matmul(
                                    pG[slot][:, :],
                                    swgT[:, D * par + 128 * dt2: D * par + 128 * (dt2 + 1)],
                                    sxnTq[:, TQ * dt2 + CH * cq: TQ * dt2 + CH * (cq + 1)],
                                    start=(dt2 == 0), stop=(dt2 == DT - 1)))
                            last = None
                            for dt2 in range(DT):
                                last = T(nc.tensor.matmul(
                                    pU[slot][:, :],
                                    swuT[:, D * par + 128 * dt2: D * par + 128 * (dt2 + 1)],
                                    sxnTq[:, TQ * dt2 + CH * cq: TQ * dt2 + CH * (cq + 1)],
                                    start=(dt2 == 0), stop=(dt2 == DT - 1)))
                            pv = Ti(last)
                            gu_pe[par] = pv
                            if pend_act is not None:
                                emit_act(*pend_act)
                            pend_act = (h6, cq, gcq, pv, grp)
                    emit_act(*pend_act)
                    pend_act = None
                    if grp + 1 < 4:
                        ht2 = 6 * (grp + 1)
                        nc.gpsimd.wait_ge(S.pe, gu_pe[0])
                        inc(nc.gpsimd.dma_start(
                            swgT[:, 0:D],
                            dd["wgT"][128 * ht2:128 * (ht2 + 1), :]), S.dF, 16)
                        pre_uv[grp + 1] = inc(nc.gpsimd.dma_start(
                            swuT[:, 0:D],
                            dd["wuT"][128 * ht2:128 * (ht2 + 1), :]), S.dF, 16)
                    nc.tensor.wait_ge(S.ve, C["_actv"])
                    for tt in range(TQ // 128):
                        for dh in range(2):
                            idx = 2 * tt + dh
                            bank = pD[idx % 2]
                            if grp * 32 + idx >= 2:
                                nc.tensor.wait_ge(S.ve, C.get("_dnv", 0))
                            last = None
                            for h6 in range(6):
                                last = T(nc.tensor.matmul(
                                    bank[:, :],
                                    sAct[:, TQ * h6 + 128 * tt: TQ * h6 + 128 * (tt + 1)],
                                    swdT[:, D * h6 + CH * dh: D * h6 + CH * (dh + 1)],
                                    start=(h6 == 0), stop=(h6 == 5)))
                            pv = Ti(last)
                            dn_grp_pe = pv
                            if pend_dn is not None:
                                emit_dn(*pend_dn)
                            pend_dn = (tt, dh, pv)
                    emit_dn(*pend_dn)
                    pend_dn = None
                barrier()

            if "ffnout" in dbg_names:
                inc(nc.sync.dma_start(d_dbg["ffnout"], sOut[:, :]), S.dO, 16)
                for eng in (nc.sync, nc.tensor, nc.vector, nc.scalar):
                    eng.wait_ge(S.dO, cur(S.dO))

            with ExitStack() as phf:
                phf.enter_context(nc.named_scope("phF_outproj"))
                Ef = phf.enter_context
                pO = [Ef(nc.psum_tensor(f"pO{i}", [128, CH], F32)) for i in range(4)]
                nc.tensor.wait_ge(S.dG, wo_v)
                for tt in range(TQ // 128):
                    for dh in range(2):
                        bidx = (2 * tt + dh) % 4
                        bank = pO[bidx]
                        if 2 * tt + dh >= 4:
                            nc.tensor.wait_ge(S.ve, C.get("_opv%d" % bidx, 0))
                        for i in range(HT):
                            T(nc.tensor.matmul(
                                bank[:, :],
                                sQC[:, TQ * i + 128 * tt: TQ * i + 128 * (tt + 1)],
                                swo[:, D * i + CH * dh: D * i + CH * (dh + 1)],
                                start=(i == 0), stop=False))
                        pv = Ti(nc.tensor.matmul(bank[:, :], sOnesR[:, 0:128],
                                                 sBo[:, CH * dh: CH * (dh + 1)],
                                                 start=False, stop=True))
                        nc.vector.wait_ge(S.pe, pv)
                        VW()
                        C["_opv"] = C["_opv%d" % bidx] = Vi(nc.vector.tensor_tensor(
                            sOut[:, D * tt + CH * dh: D * tt + CH * (dh + 1)],
                            sOut[:, D * tt + CH * dh: D * tt + CH * (dh + 1)],
                            bank[:, :], ALU.add))
                        if dh == 1:
                            nc.sync.wait_ge(S.ve, C["_opv"])
                            inc(nc.sync.dma_start(d_out[128 * tt:128 * (tt + 1), :],
                                                  sOut[:, D * tt: D * (tt + 1)]), S.dO, 16)
                nc.sync.wait_ge(S.dO, cur(S.dO))
                nc.tensor.wait_ge(S.dO, cur(S.dO))

    return nc


def kernel(L0_attn_x, L0_attn_wq_w, L0_attn_wq_b, L0_attn_wk_w, L0_attn_wk_b,
           L0_attn_wv_w, L0_attn_wv_b, L0_attn_wo_w, L0_attn_wo_b,
           L0_ffn_w_gate, L0_ffn_w_up, L0_ffn_w_down, _debug=(), _cores=None):
    x = np.ascontiguousarray(np.asarray(L0_attn_x, dtype=np.float32))
    f32c = lambda a: np.ascontiguousarray(np.asarray(a, dtype=np.float32))
    i8c = lambda a: np.ascontiguousarray(np.asarray(a).astype(np.int8))
    wq, wk, wv, wo = map(f32c, (L0_attn_wq_w, L0_attn_wk_w, L0_attn_wv_w, L0_attn_wo_w))
    bq, bk, bv, bo = (f32c(b).reshape(1, D) for b in
                      (L0_attn_wq_b, L0_attn_wk_b, L0_attn_wv_b, L0_attn_wo_b))
    def tile_gu(w):
        a = i8c(np.asarray(w).T).reshape(DT, 128, HID // 128, 128)
        return np.ascontiguousarray(a.transpose(2, 1, 0, 3).reshape(HID, D))
    wgT = tile_gu(L0_ffn_w_gate)
    wuT = tile_gu(L0_ffn_w_up)
    wdT = i8c(np.asarray(L0_ffn_w_down).T)

    tables = {p: _host_tables(p) for p in (0, 1)}
    nc = build_program(debug=_debug)
    in_maps = []
    cores = _cores if _cores is not None else list(range(8))
    for core in cores:
        b, p = core // 2, core % 2
        cosk, sink, cosq, sinq, pm, idn, mk = tables[p]
        starts = STARTS[p]
        xq = np.ascontiguousarray(np.concatenate([x[b, s:s + CH] for s in starts], 0))
        in_maps.append({
            "xkv": np.ascontiguousarray(x[b]), "xq": xq,
            "wq": wq, "wk": wk, "wv": wv, "wo": wo,
            "bq": bq, "bk": bk, "bv": bv, "bo": bo,
            "wgT": wgT, "wuT": wuT, "wdT": wdT,
            "cosk": cosk, "sink": sink, "cosq": cosq, "sinq": sinq,
            "pmat": pm, "idn": idn, "mask01": mk,
        })
    res = run_bass_kernel_spmd(nc, in_maps, core_ids=cores)
    if _debug:
        return res
    out = np.empty((B, L, D), dtype=np.float32)
    for ci, core in enumerate(cores):
        b, p = core // 2, core % 2
        o = res.results[ci]["out"]
        for j, s in enumerate(STARTS[p]):
            out[b, s:s + CH] = o[CH * j: CH * (j + 1)]
    return out
